# version 01
# baseline (speedup 1.0000x reference)
import numpy as np
from contextlib import ExitStack

import concourse.bass as bass
import concourse.mybir as mybir
from concourse.bass_utils import run_bass_kernel_spmd
from concourse.alu_op_type import AluOpType as ALU

F32 = mybir.dt.float32
BF16 = mybir.dt.bfloat16
I32 = mybir.dt.int32
AF = mybir.ActivationFunctionType
AX = mybir.AxisListType

DM = 1024
NH = 16
HD = 64
NT = 64
NU = 16
NSLOT = 16
TOK = 2048
NE = 8
HID = 4096
NG = 4
GH = 4
POLY = [-0.037, 0.481, -2.3, 4.917, -1.791]
NEG = -1.0e30


class Emitter:

    def __init__(self, nc):
        self.nc = nc

    def _reset(self):
        self.cnt = dict(sync=0, gpsimd=0, tensor=0, vector=0, scalar=0)
        self.waits = []
        self.snaps = {}
        self.wi = 0

    CHAINED = ("vector", "scalar", "gpsimd")

    def op(self, eng, fn, tick=False):
        chained = eng in self.CHAINED
        if self.mode == "count":
            if tick or chained:
                self.cnt[eng] += 1
            return
        if self.mode == eng:
            if chained:
                n = self.ecnt.get(eng, 0)
                if n > 0:
                    self.eng.wait_ge(self.sems[eng], n)
                self.ecnt[eng] = n + 1
            inst = fn(self.eng)
            if tick or chained:
                inst.then_inc(self.sems[eng], 1)

    def dma(self, queue, stream, fn):
        key = "d:" + stream
        if self.mode == "count":
            self.cnt[key] = self.cnt.get(key, 0) + 16
            return
        if self.mode == queue:
            fn(self.eng).then_inc(self.sems[key], 16)

    def wait(self, waiter, src):
        if self.mode == "count":
            self.waits.append((src, self.cnt.get(src, 0)))
            return
        src_, val = self.waits[self.wi]
        self.wi += 1
        if self.mode == waiter and val > 0:
            self.eng.wait_ge(self.sems[src_], val)

    def snap(self, key, src):
        if self.mode == "count":
            self.snaps.setdefault(key, []).append(self.cnt.get(src, 0))

    def wait_snap(self, waiter, src, key, idx):
        if idx < 0:
            return
        if self.mode == "count":
            self.waits.append((src, ("s", key, idx)))
            return
        src_, val = self.waits[self.wi]
        self.wi += 1
        if isinstance(val, tuple):
            val = self.snaps[val[1]][val[2]]
        if self.mode == waiter and val > 0:
            self.eng.wait_ge(self.sems[src_], val)

    def run(self, name, phase_fn):
        nc = self.nc
        self.mode = "count"
        self._reset()
        phase_fn(self)
        self.waits = [(s, self.snaps[v[1]][v[2]] if isinstance(v, tuple) else v)
                      for (s, v) in self.waits]
        with ExitStack() as es:
            keys = ["sync", "gpsimd", "tensor", "vector", "scalar"]
            keys += [k for k in self.cnt if k.startswith("d:")]
            sems = {k: es.enter_context(nc.semaphore(f"{name}_{k}".replace(":", "_")))
                    for k in keys}
            block = es.enter_context(nc.Block(name))

            def mk(eng_name):
                def body(eng_obj):
                    self.mode = eng_name
                    self.eng = eng_obj
                    self.sems = sems
                    self.wi = 0
                    self.ecnt = {}
                    phase_fn(self)
                return body

            block.sync(mk("sync"))
            block.gpsimd(mk("gpsimd"))
            block.tensor(mk("tensor"))
            block.vector(mk("vector"))
            block.scalar(mk("scalar"))


def build_nc():
    nc = bass.Bass("TRN2", target_bir_lowering=False)

    xe = nc.dram_tensor("xe", [8192, DM], F32, kind="ExternalInput").ap()
    wq = nc.dram_tensor("wq", [DM, DM], F32, kind="ExternalInput").ap()
    wk = nc.dram_tensor("wk", [DM, DM], F32, kind="ExternalInput").ap()
    wv = nc.dram_tensor("wv", [DM, DM], F32, kind="ExternalInput").ap()
    wo = nc.dram_tensor("wo", [DM, DM], F32, kind="ExternalInput").ap()
    bo = nc.dram_tensor("bo", [DM], F32, kind="ExternalInput").ap()
    rw = nc.dram_tensor("rw", [DM, NE], F32, kind="ExternalInput").ap()
    rb = nc.dram_tensor("rb", [NE], F32, kind="ExternalInput").ap()
    w1 = nc.dram_tensor("w1", [NE, DM, 2 * HID], F32, kind="ExternalInput").ap()
    b1 = nc.dram_tensor("b1", [NE, 2 * HID], F32, kind="ExternalInput").ap()
    w2 = nc.dram_tensor("w2", [NE, HID, DM], F32, kind="ExternalInput").ap()
    b2 = nc.dram_tensor("b2", [NE, DM], F32, kind="ExternalInput").ap()
    m1 = nc.dram_tensor("m1", [128, 512], F32, kind="ExternalInput").ap()
    m2 = nc.dram_tensor("m2", [128, 512], F32, kind="ExternalInput").ap()
    y = nc.dram_tensor("y", [TOK, DM], F32, kind="ExternalOutput").ap()

    xnf = nc.dram_tensor("xnf", [8, 128, 8192], BF16, kind="Internal").ap()
    kfm = nc.dram_tensor("kfm", [NU, 65, NH * 512], BF16, kind="Internal").ap()
    ktk = nc.dram_tensor("ktk", [NU, 128, 4, NH * 65], BF16, kind="Internal").ap()
    vtk = nc.dram_tensor("vtk", [NU, 128, 4, NH * 65], BF16, kind="Internal").ap()

    em = Emitter(nc)
    outer = ExitStack()
    sb = lambda *a: outer.enter_context(nc.sbuf_tensor(*a))

    wot = sb("wot", [128, 8, DM], BF16)
    bot = sb("bot", [128, 8], F32)
    m1s = sb("m1s", [128, 512], F32)
    m2s = sb("m2s", [128, 512], F32)
    idb = sb("idb", [128, 128], BF16)
    idf = sb("idf", [128, 128], F32)
    ones_r_b = sb("ones_r_b", [1, 128], BF16)
    ones_r_f = sb("ones_r_f", [1, 128], F32)
    ones_c_f = sb("ones_c_f", [64, 1], F32)
    ones_c_b = sb("ones_c_b", [64, 1], BF16)
    rwS = sb("rwS", [128, 8, NE], F32)
    rbS = sb("rbS", [1, NE], F32)
    maskS = sb("maskS", [128, NSLOT * NE], F32)
    stackA = ExitStack()
    sbA = lambda *a: stackA.enter_context(nc.sbuf_tensor(*a))
    wqt = sbA("wqt", [128, 8, DM], BF16)
    wkt = sbA("wkt", [128, 8, DM], BF16)
    wvt = sbA("wvt", [128, 8, DM], BF16)
    xnown = sbA("xnown", [128, 8, TOK], BF16)

    with outer:
        with ExitStack() as es:
            psb = lambda *a: es.enter_context(nc.psum_tensor(*a))
            lsb = lambda *a: es.enter_context(nc.sbuf_tensor(*a))
            iti = lsb("iti", [128, 128], I32)
            wrow = [lsb(f"wrow{i}", [128, DM], F32) for i in range(2)]
            bo8 = lsb("bo8", [8, 128], F32)
            trp = [psb(f"trp{i}", [128, 128], F32) for i in range(2)]

            def prep(em):
                em.dma("sync", "cst", lambda e: e.dma_start(m1s[:], m1))
                em.dma("sync", "cst", lambda e: e.dma_start(m2s[:], m2))
                em.dma("sync", "cst", lambda e: e.dma_start(rbS[:], rb[None, :]))
                for c in range(8):
                    em.dma("sync", "cst", lambda e, c=c: e.dma_start(
                        rwS[:, c, :], rw[c * 128:(c + 1) * 128, :]))
                em.dma("sync", "cst", lambda e: e.dma_start(
                    bo8[:], bo.rearrange("(c p) -> c p", p=128)))
                em.op("gpsimd", lambda e: e.iota(
                    iti[:], pattern=[[1, 128]], base=0, channel_multiplier=-1),
                    tick=True)
                em.op("gpsimd", lambda e: e.memset(ones_r_b[:], 1.0), tick=True)
                em.op("gpsimd", lambda e: e.memset(ones_r_f[:], 1.0), tick=True)
                em.op("gpsimd", lambda e: e.memset(ones_c_f[:], 1.0), tick=True)
                em.op("gpsimd", lambda e: e.memset(ones_c_b[:], 1.0), tick=True)
                em.wait("vector", "gpsimd")
                em.op("vector", lambda e: e.tensor_scalar(
                    idb[:], iti[:], 0, None, ALU.is_equal, ALU.bypass), tick=True)
                em.op("vector", lambda e: e.tensor_scalar(
                    idf[:], iti[:], 0, None, ALU.is_equal, ALU.bypass), tick=True)

                wlist = [(wq, wqt), (wk, wkt), (wv, wvt), (wo, wot)]
                it = 0
                for wsrc, wdst in wlist:
                    for r in range(8):
                        buf = wrow[it % 2]
                        em.wait_snap("sync", "tensor", "wrow_free", it - 2)
                        em.dma("sync", f"wrow{it % 2}", lambda e, buf=buf, wsrc=wsrc, r=r: e.dma_start(
                            buf[:], wsrc[r * 128:(r + 1) * 128, :]))
                        em.wait("tensor", f"d:wrow{it % 2}")
                        em.wait("tensor", "vector")
                        for c in range(8):
                            pt = trp[c % 2]
                            em.wait_snap("tensor", "scalar", "wtr_evac",
                                         it * 8 + c - 2)
                            em.op("tensor", lambda e, pt=pt, buf=buf, c=c: e.transpose(
                                pt[:], buf[:, c * 128:(c + 1) * 128], idf[:]),
                                tick=True)
                            em.wait("scalar", "tensor")
                            em.op("scalar", lambda e, pt=pt, wdst=wdst, c=c, r=r: e.copy(
                                wdst[:, c, r * 128:(r + 1) * 128], pt[:]), tick=True)
                            em.snap("wtr_evac", "scalar")
                        em.snap("wrow_free", "tensor")
                        it += 1
                em.wait("tensor", "d:cst")
                em.wait_snap("tensor", "scalar", "wtr_evac", 8 * 32 - 2)
                em.op("tensor", lambda e: e.transpose(
                    trp[0][0:128, 0:8], bo8[:], idf[0:8, 0:8]), tick=True)
                em.wait("scalar", "tensor")
                em.op("scalar", lambda e: e.copy(bot[:], trp[0][0:128, 0:8]),
                      tick=True)
                em.wait("sync", "d:cst")

            em.run("prep", prep)

        with ExitStack() as es:
            psb = lambda *a: es.enter_context(nc.psum_tensor(*a))
            lsb = lambda *a: es.enter_context(nc.sbuf_tensor(*a))
            xT = [lsb(f"xT{i}", [128, DM], F32) for i in range(2)]
            ssum = lsb("ssum", [128, 1], F32)
            nmean = lsb("nmean", [128, 1], F32)
            ssq = lsb("ssq", [128, 1], F32)
            vstd = lsb("vstd", [128, 1], F32)
            rstd = lsb("rstd", [128, 1], F32)
            sqj = lsb("sqj", [128, DM], F32)
            xnb = [lsb(f"xnb{i}", [128, DM], BF16) for i in range(2)]
            xn32 = lsb("xn32", [128, DM], F32)
            xf32 = lsb("xf32", [128, 8, 128], F32)
            xfst = [lsb(f"xfst{i}", [128, 8, 512], BF16) for i in range(2)]
            rmax8 = lsb("rmax8", [128, 1], F32)
            rraw = lsb("rraw", [128, 1], F32)
            ravail = lsb("ravail", [128, 1], F32)
            rtaken = lsb("rtaken", [128, 1], F32)
            trb = psb("trb", [128, DM], BF16)
            trf = psb("trf", [128, DM], F32)
            lp = psb("lp", [128, NE], F32)

            def p0(em):
                for p in range(NT):
                    u, r = p // 4, p % 4
                    own = (r == 0)
                    slot = u
                    buf = xT[p % 2]
                    em.wait_snap("sync", "vector", "xT_free", p - 2)
                    em.dma("sync", f"x{p % 2}", lambda e, buf=buf, p=p: e.dma_start(
                        buf[:], xe[p * 128:(p + 1) * 128, :]))
                    em.wait("vector", f"d:x{p % 2}")
                    em.wait_snap("vector", "scalar", "sqrt_done", p - 1)
                    em.op("vector", lambda e, buf=buf: e.reduce_sum(
                        ssum[:], buf[:], axis=AX.X))
                    em.op("vector", lambda e: e.tensor_scalar(
                        nmean[:], ssum[:], -1.0 / DM, None, ALU.mult, ALU.bypass),
                        tick=True)
                    em.wait("scalar", "vector")
                    em.op("scalar", lambda e, buf=buf: e.activation(
                        sqj[:], buf[:], AF.Square, bias=nmean[:], scale=1.0,
                        accum_out=ssq[:]), tick=True)
                    em.wait("vector", "scalar")
                    em.op("vector", lambda e: e.tensor_scalar(
                        vstd[:], ssq[:], 1.0 / DM, 1.0e-5, ALU.mult, ALU.add),
                        tick=True)
                    em.wait("scalar", "vector")
                    em.op("scalar", lambda e: e.activation(
                        vstd[:], vstd[:], AF.Sqrt), tick=True)
                    em.snap("sqrt_done", "scalar")
                    em.wait("vector", "scalar")
                    em.op("vector", lambda e: e.reciprocal(rstd[:], vstd[:]))
                    xb = xnb[p % 2]
                    em.wait_snap("vector", "tensor", "xnb_free", p - 2)
                    em.op("vector", lambda e, buf=buf, xb=xb: e.tensor_scalar(
                        xb[:], buf[:], nmean[:], rstd[:], ALU.add, ALU.mult),
                        tick=True)
                    if own:
                        em.wait_snap("vector", "tensor", "trf_done", slot - 1)
                        em.op("vector", lambda e, buf=buf: e.tensor_scalar(
                            xn32[:], buf[:], nmean[:], rstd[:], ALU.add, ALU.mult),
                            tick=True)
                    em.snap("xT_free", "vector")
                    em.wait("tensor", "vector")
                    em.wait_snap("tensor", "scalar", "trb_evac", p - 1)
                    for c in range(8):
                        em.op("tensor", lambda e, xb=xb, c=c: e.transpose(
                            trb[:, c * 128:(c + 1) * 128],
                            xb[:, c * 128:(c + 1) * 128], idb[:]), tick=(c == 7))
                    em.snap("xnb_free", "tensor")
                    st = xfst[u % 2]
                    em.wait("scalar", "tensor")
                    if r == 0:
                        em.wait_snap("scalar", f"d:xf{u % 2}", "xfst_sent", u - 2)
                    em.op("scalar", lambda e, st=st, r=r: e.copy(
                        st[:, :, r * 128:(r + 1) * 128],
                        trb[:].rearrange("p (c t) -> p c t", c=8)), tick=True)
                    if own:
                        em.op("scalar", lambda e, slot=slot: e.copy(
                            xnown[:, :, slot * 128:(slot + 1) * 128],
                            trb[:].rearrange("p (c t) -> p c t", c=8)), tick=True)
                    em.snap("trb_evac", "scalar")
                    if own:
                        em.wait("tensor", "vector")
                        em.wait_snap("tensor", "scalar", "trf_evac", slot - 1)
                        for c in range(8):
                            em.op("tensor", lambda e, c=c: e.transpose(
                                trf[:, c * 128:(c + 1) * 128],
                                xn32[:, c * 128:(c + 1) * 128], idf[:]),
                                tick=(c == 7))
                        em.snap("trf_done", "tensor")
                        em.wait("scalar", "tensor")
                        em.op("scalar", lambda e: e.copy(
                            xf32[:], trf[:].rearrange("p (c t) -> p c t", c=8)),
                            tick=True)
                        em.snap("trf_evac", "scalar")
                        em.wait("tensor", "scalar")
                        em.wait_snap("tensor", "vector", "lp_done", slot - 1)
                        for c in range(8):
                            em.op("tensor", lambda e, c=c: e.matmul(
                                lp[:], xf32[:, c, :], rwS[:, c, :],
                                start=(c == 0), stop=False))
                        em.op("tensor", lambda e: e.matmul(
                            lp[:], ones_r_f[:], rbS[:], start=False, stop=True),
                            tick=True)
                        em.wait("vector", "tensor")
                        em.op("vector", lambda e: e.reduce_max(
                            rmax8[:], lp[:], axis=AX.X))
                        em.op("vector", lambda e: e.memset(rtaken[:], 0.0))
                        for ei in range(NE):
                            mcol = maskS[:, slot * NE + ei:slot * NE + ei + 1]
                            em.op("vector", lambda e, ei=ei: e.tensor_tensor(
                                rraw[:], lp[:, ei:ei + 1], rmax8[:], ALU.is_ge))
                            em.op("vector", lambda e: e.tensor_scalar(
                                ravail[:], rtaken[:], -1.0, 1.0, ALU.mult, ALU.add))
                            em.op("vector", lambda e, mcol=mcol: e.tensor_tensor(
                                mcol, rraw[:], ravail[:], ALU.mult))
                            em.op("vector", lambda e, mcol=mcol: e.tensor_tensor(
                                rtaken[:], rtaken[:], mcol, ALU.max),
                                tick=(ei == NE - 1))
                        em.snap("lp_done", "vector")
                    if r == 3:
                        em.wait("sync", "scalar")
                        em.dma("sync", f"xf{u % 2}", lambda e, st=st, u=u: e.dma_start(
                            xnf[:, :, u * 512:(u + 1) * 512]
                            .rearrange("c p t -> p c t"), st[:]))
                        em.snap("xfst_sent", f"d:xf{u % 2}")
                em.wait("sync", "d:xf0")
                em.wait("sync", "d:xf1")

            em.run("p0", p0)

        with ExitStack() as es:
            psb = lambda *a: es.enter_context(nc.psum_tensor(*a))
            lsb = lambda *a: es.enter_context(nc.sbuf_tensor(*a))
            xkb = [lsb(f"xkb{i}", [128, 8, 512], BF16) for i in range(2)]
            kstg = [lsb(f"kstg{i}", [65, NH * 512], BF16) for i in range(2)]
            ktst = [lsb(f"ktst{i}", [128, 4, NH * 65], BF16) for i in range(2)]
            vtst = [lsb(f"vtst{i}", [128, 4, NH * 65], BF16) for i in range(2)]
            kfp = [psb(f"kfp{i}", [128, 512], F32) for i in range(2)]
            ktp = psb("ktp", [128, DM], F32)
            vtp = psb("vtp", [128, DM], F32)

            def kv(em):
                for i in range(2):
                    em.op("gpsimd", lambda e, i=i: e.memset(
                        kstg[i][64:65, :], 1.0), tick=True)
                    em.op("gpsimd", lambda e, i=i: e.memset(
                        ktst[i][:].rearrange("p t (h o) -> p t h o", o=65)
                        [:, :, :, 64:65], 1.0), tick=True)
                    em.op("gpsimd", lambda e, i=i: e.memset(
                        vtst[i][:].rearrange("p t (h o) -> p t h o", o=65)
                        [:, :, :, 64:65], 1.0), tick=True)
                for u in range(NU):
                    xb = xkb[u % 2]
                    em.wait_snap("sync", "tensor", "xkb_free", u - 2)
                    em.dma("sync", f"xkb{u % 2}", lambda e, xb=xb, u=u: e.dma_start(
                        xb[:], xnf[:, :, u * 512:(u + 1) * 512]
                        .rearrange("c p t -> p c t")))
                    ks = kstg[u % 2]
                    em.wait("tensor", f"d:xkb{u % 2}")
                    for hp in range(8):
                        kp = kfp[hp % 2]
                        em.wait_snap("tensor", "scalar", "kfp_evac", u * 8 + hp - 2)
                        for c in range(8):
                            em.op("tensor", lambda e, kp=kp, c=c, hp=hp, xb=xb: e.matmul(
                                kp[:], wkt[:, c, hp * 128:(hp + 1) * 128],
                                xb[:, c, :], start=(c == 0), stop=(c == 7)),
                                tick=(c == 7))
                        em.wait("scalar", "tensor")
                        if hp == 0:
                            em.wait("scalar", "gpsimd")
                            em.wait_snap("scalar", f"d:kst{u % 2}", "kfm_sent", u - 2)
                        em.op("scalar", lambda e, kp=kp, ks=ks, hp=hp: e.copy(
                            ks[0:64, (2 * hp) * 512:(2 * hp + 1) * 512], kp[0:64, :]))
                        em.op("scalar", lambda e, kp=kp, ks=ks, hp=hp: e.copy(
                            ks[0:64, (2 * hp + 1) * 512:(2 * hp + 2) * 512],
                            kp[64:128, :]), tick=True)
                        em.snap("kfp_evac", "scalar")
                    em.wait("sync", "scalar")
                    em.dma("sync", f"kst{u % 2}", lambda e, ks=ks, u=u: e.dma_start(kfm[u], ks[:]))
                    em.snap("kfm_sent", f"d:kst{u % 2}")
                    kt = ktst[u % 2]
                    for t in range(4):
                        em.wait_snap("tensor", "scalar", "ktp_evac", u * 4 + t - 1)
                        for c in range(8):
                            for hh in range(2):
                                em.op("tensor", lambda e, xb=xb, c=c, t=t, hh=hh: e.matmul(
                                    ktp[:, hh * 512:(hh + 1) * 512],
                                    xb[:, c, t * 128:(t + 1) * 128],
                                    wkt[:, c, hh * 512:(hh + 1) * 512],
                                    start=(c == 0), stop=(c == 7)),
                                    tick=(c == 7 and hh == 1))
                        em.wait("scalar", "tensor")
                        if t == 0:
                            em.wait_snap("scalar", f"d:ktt{u % 2}", "ktk_sent", u - 2)
                        em.op("scalar", lambda e, kt=kt, t=t: e.copy(
                            kt[:].rearrange("p t (h o) -> p t h o", o=65)
                            [:, t, :, 0:64],
                            ktp[:].rearrange("p (h d) -> p h d", d=64)), tick=True)
                        em.snap("ktp_evac", "scalar")
                    em.wait("sync", "scalar")
                    em.dma("sync", f"ktt{u % 2}", lambda e, kt=kt, u=u: e.dma_start(ktk[u], kt[:]))
                    em.snap("ktk_sent", f"d:ktt{u % 2}")
                    vt = vtst[u % 2]
                    for t in range(4):
                        em.wait_snap("tensor", "scalar", "vtp_evac", u * 4 + t - 1)
                        for c in range(8):
                            for hh in range(2):
                                em.op("tensor", lambda e, xb=xb, c=c, t=t, hh=hh: e.matmul(
                                    vtp[:, hh * 512:(hh + 1) * 512],
                                    xb[:, c, t * 128:(t + 1) * 128],
                                    wvt[:, c, hh * 512:(hh + 1) * 512],
                                    start=(c == 0), stop=(c == 7)),
                                    tick=(c == 7 and hh == 1))
                        em.wait("scalar", "tensor")
                        if t == 0:
                            em.wait_snap("scalar", f"d:vtt{u % 2}", "vtk_sent", u - 2)
                        em.op("scalar", lambda e, vt=vt, t=t: e.copy(
                            vt[:].rearrange("p t (h o) -> p t h o", o=65)
                            [:, t, :, 0:64],
                            vtp[:].rearrange("p (h d) -> p h d", d=64)), tick=True)
                        em.snap("vtp_evac", "scalar")
                    em.snap("xkb_free", "tensor")
                    em.wait("sync", "scalar")
                    em.dma("sync", f"vtt{u % 2}", lambda e, vt=vt, u=u: e.dma_start(vtk[u], vt[:]))
                    em.snap("vtk_sent", f"d:vtt{u % 2}")
                for s_ in ("kst0", "kst1", "ktt0", "ktt1", "vtt0", "vtt1"):
                    em.wait("sync", "d:" + s_)

            em.run("kv", kv)

        stackQ = ExitStack()
        qbuf = stackQ.enter_context(nc.sbuf_tensor("qbuf", [65, NH * TOK], BF16,
                                                   side="right"))
        with ExitStack() as es:
            psb = lambda *a: es.enter_context(nc.psum_tensor(*a))
            qps = [psb(f"qps{i}", [128, 512], F32) for i in range(2)]

            def attn_q(em):
                for hp in range(8):
                    for tc in range(4):
                        it = hp * 4 + tc
                        qp = qps[it % 2]
                        em.wait_snap("tensor", "scalar", "qp_evac", it - 2)
                        for c in range(8):
                            em.op("tensor", lambda e, qp=qp, c=c, hp=hp, tc=tc: e.matmul(
                                qp[:], wqt[:, c, hp * 128:(hp + 1) * 128],
                                xnown[:, c, tc * 512:(tc + 1) * 512],
                                start=(c == 0), stop=(c == 7)), tick=(c == 7))
                        em.wait("scalar", "tensor")
                        h0, h1 = 2 * hp, 2 * hp + 1
                        em.op("scalar", lambda e, qp=qp, h0=h0, tc=tc: e.activation(
                            qbuf[0:64, h0 * TOK + tc * 512:h0 * TOK + (tc + 1) * 512],
                            qp[0:64, :], AF.Copy, scale=0.125))
                        em.op("scalar", lambda e, qp=qp, h1=h1, tc=tc: e.activation(
                            qbuf[0:64, h1 * TOK + tc * 512:h1 * TOK + (tc + 1) * 512],
                            qp[64:128, :], AF.Copy, scale=0.125), tick=True)
                        em.snap("qp_evac", "scalar")

            em.run("attn_q", attn_q)
        stackA.close()

        with ExitStack() as es:
            psb = lambda *a: es.enter_context(nc.psum_tensor(*a))
            lsb = lambda *a: es.enter_context(nc.sbuf_tensor(*a))
            sps = [psb(f"sps{i}", [128, 512], F32) for i in range(2)]
            mtp = psb("mtp", [16, 128], F32)
            kbf = [lsb(f"kbf{i}", [65, NH * 512], BF16) for i in range(2)]
            mtr = lsb("mtr", [16, 128], BF16)
            rmat = lsb("rmat", [128, NSLOT * NH * NU], BF16)
            mmat = lsb("mmat", [128, NSLOT * NH], F32)

            def attn_a(em):
                em.op("vector", lambda e: e.memset(rmat[:], -3.0e38), tick=True)
                nsp = 0
                for u in range(NU):
                    kb = kbf[u % 2]
                    em.wait_snap("sync", "tensor", "kbf_free", u - 2)
                    em.dma("sync", f"kbf{u % 2}", lambda e, kb=kb, u=u: e.dma_start(kb[:], kfm[u]))
                    em.wait("tensor", f"d:kbf{u % 2}")
                    em.wait("tensor", "vector")
                    em.wait("tensor", "scalar")
                    for h in range(NH):
                        for i in range(u, NSLOT):
                            sp = sps[nsp % 2]
                            em.wait_snap("tensor", "vector", "sps_done", nsp - 2)
                            em.op("tensor", lambda e, sp=sp, h=h, i=i, kb=kb: e.matmul(
                                sp[:], qbuf[0:64, h * TOK + i * 128:h * TOK + (i + 1) * 128],
                                kb[0:64, h * 512:(h + 1) * 512],
                                start=True, stop=True), tick=True)
                            em.wait("vector", "tensor")
                            if i == u:
                                em.op("vector", lambda e, sp=sp: e.tensor_tensor(
                                    sp[:], sp[:], m1s[:], ALU.add))
                            col = (i * NH + h) * NU + u
                            em.op("vector", lambda e, sp=sp, col=col: e.reduce_max(
                                rmat[:, col:col + 1], sp[:], axis=AX.X), tick=True)
                            em.snap("sps_done", "vector")
                            nsp += 1
                    em.snap("kbf_free", "tensor")
                for i in range(NSLOT):
                    for h in range(NH):
                        cb = (i * NH + h) * NU
                        em.op("vector", lambda e, cb=cb, i=i, h=h: e.reduce_max(
                            mmat[:, i * NH + h:i * NH + h + 1],
                            rmat[:, cb:cb + NU], axis=AX.X), tick=(h == NH - 1))
                for i in range(NSLOT):
                    em.wait("tensor", "vector")
                    em.wait_snap("tensor", "scalar", "mtr_evac", i - 1)
                    em.op("tensor", lambda e, i=i: e.transpose(
                        mtp[:], mmat[:, i * NH:(i + 1) * NH], idf[:]), tick=True)
                    em.wait("scalar", "tensor")
                    em.wait_snap("scalar", "d:mtr", "mtr_sent", i - 1)
                    em.op("scalar", lambda e: e.activation(
                        mtr[:], mtp[:], AF.Copy, scale=-1.0), tick=True)
                    em.snap("mtr_evac", "scalar")
                    em.wait("sync", "scalar")
                    em.dma("sync", "mtr", lambda e, i=i: e.dma_start(
                        qbuf[64:65, :].rearrange("o (h t) -> o h t", t=TOK)
                        [:, :, i * 128:(i + 1) * 128],
                        mtr[:]))
                    em.snap("mtr_sent", "d:mtr")
                em.wait("sync", "d:mtr")

            em.run("attn_a", attn_a)

        stackC = ExitStack()
        ntbig = stackC.enter_context(nc.sbuf_tensor("ntbig", [128, 8 * TOK], BF16))
        with ExitStack() as es:
            psb = lambda *a: es.enter_context(nc.psum_tensor(*a))
            lsb = lambda *a: es.enter_context(nc.sbuf_tensor(*a))
            sps = [psb(f"s2ps{i}", [128, 512], F32) for i in range(2)]
            ozp = [psb(f"ozp{i}", [65, 512], F32) for i in range(2)]
            urp = [psb(f"urp{i}", [1, 512], F32) for i in range(2)]
            bcp = [psb(f"bcp{i}", [64, 512], F32) for i in range(2)]
            kb2 = [lsb(f"kb2{i}", [65, GH * 512], BF16) for i in range(2)]
            ntb_ = [lsb(f"ntb{i}", [128, 4, GH * 65], BF16) for i in range(2)]
            ttl = [lsb(f"ttl{i}", [128, 4, 512], BF16) for i in range(2)]
            uza = lsb("uza", [65, GH * TOK], F32)
            prod = lsb("prod", [64, TOK], BF16)
            rowA = lsb("rowA", [1, TOK], F32)
            rowB = lsb("rowB", [1, TOK], F32)
            rowC = lsb("rowC", [1, TOK], F32)
            rowM = lsb("rowM", [1, TOK], F32)

            def chunks_of(u):
                out, off, nq = [], u * 128, TOK - u * 128
                while nq > 0:
                    w = min(512, nq)
                    out.append((off, w))
                    off += w
                    nq -= w
                return out

            def attn_b(em):
                state = dict(nbc=0, nur=0)

                def sweep(g, which, sw):
                    em.wait("vector", "scalar")
                    em.wait("sync", "tensor")
                    em.op("vector", lambda e: e.memset(uza[:], 0.0), tick=True)
                    nt = 0
                    nz = 0
                    nsp = 0
                    for u in range(NU):
                        kb = kb2[u % 2]
                        src = ktk if which == "uz" else vtk
                        ntb = ntb_[u % 2]
                        em.wait_snap("sync", "tensor", f"kb2f_{sw}", u - 2)
                        em.dma("sync", f"kbn{u % 2}", lambda e, kb=kb, u=u, g=g: e.dma_start(
                            kb[:], kfm[u].rearrange("o (h t) -> o h t", t=512)
                            [:, 4 * g:4 * g + 4, :]))
                        em.dma("sync", f"kbn{u % 2}", lambda e, ntb=ntb, u=u, g=g, src=src: e.dma_start(
                            ntb[:], src[u].rearrange("p t (h o) -> p t h o", o=65)
                            [:, :, 4 * g:4 * g + 4, :]))
                        em.wait("tensor", f"d:kbn{u % 2}")
                        for hl in range(GH):
                            h = 4 * g + hl
                            for ci, (qoff, qw) in enumerate(chunks_of(u)):
                                tb = ttl[nt % 2]
                                for t in range(4):
                                    sp = sps[nsp % 2]
                                    em.wait_snap("tensor", "scalar", f"exp_{sw}",
                                                 nsp - 2)
                                    em.op("tensor", lambda e, sp=sp, kb=kb, hl=hl,
                                          t=t, qoff=qoff, qw=qw, h=h: e.matmul(
                                              sp[:, 0:qw],
                                              kb[0:65, hl * 512 + t * 128:
                                                 hl * 512 + (t + 1) * 128],
                                              qbuf[0:65, h * TOK + qoff:
                                                   h * TOK + qoff + qw],
                                              start=True, stop=True), tick=True)
                                    if t == 0:
                                        em.wait_snap("scalar", "tensor",
                                                     f"ozmm_{sw}", nt - 2)
                                    if ci == 0:
                                        em.wait("vector", "tensor")
                                        em.op("vector", lambda e, sp=sp, t=t: e.tensor_tensor(
                                            sp[:, 0:128], sp[:, 0:128],
                                            m2s[:, t * 128:(t + 1) * 128], ALU.add),
                                            tick=True)
                                        em.wait("scalar", "vector")
                                    else:
                                        em.wait("scalar", "tensor")
                                    em.op("scalar", lambda e, sp=sp, tb=tb, t=t,
                                          qw=qw: e.activation(
                                              tb[:, t, 0:qw], sp[:, 0:qw], AF.Exp),
                                          tick=True)
                                    em.snap(f"exp_{sw}", "scalar")
                                    nsp += 1
                                oz = ozp[nz % 2]
                                em.wait_snap("tensor", "vector", f"oz_{sw}", nz - 2)
                                em.wait("tensor", "scalar")
                                for t in range(4):
                                    em.op("tensor", lambda e, oz=oz, ntb=ntb, tb=tb,
                                          t=t, hl=hl, qw=qw: e.matmul(
                                              oz[:, 0:qw],
                                              ntb[:, t, hl * 65:(hl + 1) * 65],
                                              tb[:, t, 0:qw],
                                              start=(t == 0), stop=(t == 3)),
                                          tick=(t == 3))
                                em.snap(f"ozmm_{sw}", "tensor")
                                em.wait("vector", "tensor")
                                em.op("vector", lambda e, oz=oz, hl=hl, qoff=qoff,
                                      qw=qw: e.tensor_tensor(
                                          uza[0:65, hl * TOK + qoff:hl * TOK + qoff + qw],
                                          uza[0:65, hl * TOK + qoff:hl * TOK + qoff + qw],
                                          oz[:, 0:qw], ALU.add), tick=True)
                                em.snap(f"oz_{sw}", "vector")
                                nt += 1
                                nz += 1
                        em.snap(f"kb2f_{sw}", "tensor")

                for g in range(NG):
                    sweep(g, "uz", 2 * g)
                    for hl in range(GH):
                        h = 4 * g + hl
                        hc = hl * TOK
                        em.wait("vector", "tensor")
                        em.op("vector", lambda e, h=h, hc=hc: e.tensor_tensor(
                            prod[:], qbuf[0:64, h * TOK:(h + 1) * TOK],
                            uza[0:64, hc:hc + TOK], ALU.mult), tick=True)
                        for c in range(4):
                            ur = urp[c % 2]
                            em.wait("tensor", "vector")
                            em.wait_snap("tensor", "scalar", "ur_evac",
                                         state["nur"] - 2)
                            em.op("tensor", lambda e, ur=ur, c=c: e.matmul(
                                ur[:], ones_c_b[:], prod[0:64, c * 512:(c + 1) * 512],
                                start=True, stop=True), tick=True)
                            em.wait("scalar", "tensor")
                            em.op("scalar", lambda e, ur=ur, c=c: e.copy(
                                rowA[:, c * 512:(c + 1) * 512], ur[:]), tick=True)
                            em.snap("ur_evac", "scalar")
                            state["nur"] += 1
                        em.wait("vector", "scalar")
                        em.op("vector", lambda e, hc=hc: e.reciprocal(
                            rowB[:], uza[64:65, hc:hc + TOK]), tick=True)
                        em.wait("scalar", "vector")
                        em.op("scalar", lambda e, hc=hc: e.activation(
                            rowC[:], uza[64:65, hc:hc + TOK], AF.Ln), tick=True)
                        em.wait("vector", "scalar")
                        em.op("vector", lambda e: e.tensor_tensor(
                            rowA[:], rowA[:], rowB[:], ALU.mult))
                        em.op("vector", lambda e: e.tensor_tensor(
                            rowA[:], rowC[:], rowA[:], ALU.subtract))
                        em.op("scalar", lambda e, h=h: e.copy(
                            rowM[:], qbuf[64:65, h * TOK:(h + 1) * TOK]), tick=True)
                        em.wait("vector", "scalar")
                        em.op("vector", lambda e: e.tensor_tensor(
                            rowA[:], rowA[:], rowM[:], ALU.subtract))
                        em.op("vector", lambda e: e.tensor_tensor(
                            rowB[:], rowA[:], rowA[:], ALU.mult))
                        em.op("vector", lambda e: e.tensor_scalar(
                            rowC[:], rowB[:], POLY[0], POLY[2], ALU.mult, ALU.add))
                        em.op("vector", lambda e: e.tensor_tensor(
                            rowC[:], rowC[:], rowB[:], ALU.mult))
                        em.op("vector", lambda e: e.tensor_scalar(
                            rowB[:], rowB[:], POLY[1], POLY[3], ALU.mult, ALU.add))
                        em.op("vector", lambda e: e.tensor_tensor(
                            rowB[:], rowB[:], rowA[:], ALU.mult))
                        em.op("vector", lambda e: e.tensor_tensor(
                            rowC[:], rowC[:], rowB[:], ALU.add))
                        em.op("vector", lambda e: e.tensor_scalar(
                            rowC[:], rowC[:], POLY[4], 1.0, ALU.add, ALU.max))
                        em.op("vector", lambda e: e.tensor_scalar(
                            rowB[:], rowA[:], 0.5, None, ALU.is_gt, ALU.bypass))
                        em.op("vector", lambda e: e.tensor_scalar(
                            rowC[:], rowC[:], -1.0, None, ALU.add, ALU.bypass))
                        em.op("vector", lambda e: e.tensor_tensor(
                            rowC[:], rowB[:], rowC[:], ALU.mult))
                        em.op("vector", lambda e: e.tensor_scalar(
                            rowC[:], rowC[:], 1.0, None, ALU.add, ALU.bypass),
                            tick=True)
                        for c in range(4):
                            bc = bcp[c % 2]
                            em.wait("tensor", "vector")
                            em.wait_snap("tensor", "vector", "bcu", state["nbc"] - 2)
                            em.op("tensor", lambda e, bc=bc, c=c: e.matmul(
                                bc[:], ones_r_f[0:1, 0:64],
                                rowC[:, c * 512:(c + 1) * 512],
                                start=True, stop=True), tick=True)
                            em.wait("vector", "tensor")
                            em.op("vector", lambda e, bc=bc, h=h, c=c: e.tensor_tensor(
                                qbuf[0:64, h * TOK + c * 512:h * TOK + (c + 1) * 512],
                                qbuf[0:64, h * TOK + c * 512:h * TOK + (c + 1) * 512],
                                bc[:], ALU.mult), tick=True)
                            em.snap("bcu", "vector")
                            state["nbc"] += 1
                        em.op("vector", lambda e: e.tensor_tensor(
                            rowM[:], rowM[:], rowC[:], ALU.mult), tick=True)
                        em.wait("scalar", "vector")
                        em.op("scalar", lambda e, h=h: e.copy(
                            qbuf[64:65, h * TOK:(h + 1) * TOK], rowM[:]), tick=True)
                    em.wait("tensor", "vector")
                    sweep(g, "av", 2 * g + 1)
                    for hl in range(GH):
                        h = 4 * g + hl
                        hc = hl * TOK
                        em.wait("vector", "tensor")
                        em.op("vector", lambda e, hc=hc: e.reciprocal(
                            rowB[:], uza[64:65, hc:hc + TOK]), tick=True)
                        for c in range(4):
                            bc = bcp[c % 2]
                            em.wait("tensor", "vector")
                            em.wait_snap("tensor", "vector", "bcu", state["nbc"] - 2)
                            em.op("tensor", lambda e, bc=bc, c=c: e.matmul(
                                bc[:], ones_r_f[0:1, 0:64],
                                rowB[:, c * 512:(c + 1) * 512],
                                start=True, stop=True), tick=True)
                            em.wait("vector", "tensor")
                            em.op("vector", lambda e, bc=bc, h=h, hc=hc, c=c: e.tensor_tensor(
                                ntbig[(h % 2) * 64:(h % 2) * 64 + 64,
                                      (h // 2) * TOK + c * 512:(h // 2) * TOK + (c + 1) * 512],
                                uza[0:64, hc + c * 512:hc + (c + 1) * 512],
                                bc[:], ALU.mult), tick=True)
                            em.snap("bcu", "vector")
                            state["nbc"] += 1

            em.run("attn_b", attn_b)
        stackQ.close()
        stackAcc = ExitStack()
        acctok = stackAcc.enter_context(
            nc.sbuf_tensor("acctok", [128, NSLOT, DM], BF16, side="right"))

        with ExitStack() as es:
            psb = lambda *a: es.enter_context(nc.psum_tensor(*a))
            lsb = lambda *a: es.enter_context(nc.sbuf_tensor(*a))
            wps = [psb(f"wps{i}", [128, 512], F32) for i in range(2)]
            atp = [psb(f"atp{i}", [128, 512], F32) for i in range(2)]
            afs = [lsb(f"afs{i}", [128, 512], F32) for i in range(2)]

            def wo_phase(em):
                for dc in range(8):
                    for tc in range(4):
                        it = dc * 4 + tc
                        wp = wps[it % 2]
                        em.wait_snap("tensor", "scalar", "wp_evac", it - 2)
                        for pc in range(8):
                            em.op("tensor", lambda e, wp=wp, pc=pc, dc=dc, tc=tc: e.matmul(
                                wp[:], wot[:, pc, dc * 128:(dc + 1) * 128],
                                ntbig[:, pc * TOK + tc * 512:pc * TOK + (tc + 1) * 512],
                                start=(pc == 0), stop=(pc == 7)), tick=(pc == 7))
                        af = afs[it % 2]
                        em.wait("scalar", "tensor")
                        em.wait_snap("scalar", "tensor", "afs_used", it - 2)
                        em.op("scalar", lambda e, wp=wp, af=af, dc=dc: e.activation(
                            af[:], wp[:], AF.Identity, bias=bot[:, dc:dc + 1]),
                            tick=True)
                        em.snap("wp_evac", "scalar")
                        at = atp[it % 2]
                        em.wait("tensor", "scalar")
                        em.wait_snap("tensor", "scalar", "atp_evac", it - 2)
                        for s in range(4):
                            em.op("tensor", lambda e, at=at, af=af, s=s: e.transpose(
                                at[:, s * 128:(s + 1) * 128],
                                af[:, s * 128:(s + 1) * 128], idf[:]), tick=(s == 3))
                        em.snap("afs_used", "tensor")
                        em.wait("scalar", "tensor")
                        em.op("scalar", lambda e, at=at, dc=dc, tc=tc: e.copy(
                            acctok[:, tc * 4:tc * 4 + 4, dc * 128:(dc + 1) * 128],
                            at[:].rearrange("p (t q) -> p t q", t=4)), tick=True)
                        em.snap("atp_evac", "scalar")

            em.run("wo", wo_phase)
        stackC.close()

        with ExitStack() as es:
            psb = lambda *a: es.enter_context(nc.psum_tensor(*a))
            lsb = lambda *a: es.enter_context(nc.sbuf_tensor(*a))
            w1q = [lsb(f"w1q{i}", [128, 8, 2048], BF16) for i in range(2)]
            xnown = lsb("xnown2", [128, 8, TOK], BF16)
            w2q = [lsb(f"w2q{i}", [128, 8, DM], BF16) for i in range(2)]
            b1ta = lsb("b1ta", [128, 32], F32)
            b1tg = lsb("b1tg", [128, 32], F32)
            b1st = lsb("b1st", [32, 128], F32)
            b2r = [lsb(f"b2r{i}", [1, DM], BF16) for i in range(2)]
            sgb = [lsb(f"sgb{i}", [128, 256], BF16) for i in range(2)]
            ubf = [lsb(f"ubf{i}", [128, 256], BF16) for i in range(2)]
            aps = [psb(f"aps{i}", [128, 256], F32) for i in range(2)]
            gps = [psb(f"gps{i}", [128, 256], F32) for i in range(2)]
            yps = [psb(f"yps{i}", [128, DM], F32) for i in range(2)]

            def ffn(em):
                for i in range(NSLOT):
                    em.dma("sync", "xn2", lambda e, i=i: e.dma_start(
                        xnown[:, :, i * 128:(i + 1) * 128],
                        xnf[:, :, (4 * i) * 128:(4 * i) * 128 + 128]
                        .rearrange("c p t -> p c t")))
                nq = 0
                nhc = 0
                ntc = 0
                for ei in range(NE):
                    for half in range(2):
                        bidx = 2 * ei + half
                        em.wait_snap("sync", "tensor", "b1tr", bidx - 1)
                        em.dma("sync", "b1s", lambda e, ei=ei, half=half: e.dma_start(
                            b1st[:], b1[ei, half * HID:(half + 1) * HID]
                            .rearrange("(r f) -> r f", f=128)))
                        em.wait("tensor", "d:b1s")
                        em.wait("tensor", "scalar")
                        em.op("tensor", lambda e: e.transpose(
                            gps[0][0:128, 0:32], b1st[:], idf[0:32, 0:32]), tick=True)
                        em.snap("b1tr", "tensor")
                        em.wait("scalar", "tensor")
                        em.op("scalar", lambda e, half=half: e.copy(
                            (b1ta if half == 0 else b1tg)[:], gps[0][0:128, 0:32]),
                            tick=True)
                    bb = b2r[ei % 2]
                    em.wait_snap("gpsimd", "tensor", "b2_used", ei - 2)
                    em.dma("gpsimd", f"b2s{ei % 2}", lambda e, bb=bb, ei=ei: e.dma_start(
                        bb[:], b2[ei][None, :]))
                    for qt in range(4):
                        wa = w1q[nq % 2]
                        wb = w2q[nq % 2]
                        em.wait_snap("gpsimd", "tensor", "wq_free", nq - 2)
                        em.dma("gpsimd", f"wqs{nq % 2}", lambda e, wa=wa, ei=ei, qt=qt: e.dma_start(
                            wa[:, :, 0:1024],
                            w1[ei].rearrange("(c p) n -> p c n", p=128)
                            [:, :, qt * 1024:(qt + 1) * 1024]))
                        em.dma("gpsimd", f"wqs{nq % 2}", lambda e, wa=wa, ei=ei, qt=qt: e.dma_start(
                            wa[:, :, 1024:2048],
                            w1[ei].rearrange("(c p) n -> p c n", p=128)
                            [:, :, HID + qt * 1024:HID + (qt + 1) * 1024]))
                        em.dma("gpsimd", f"wqs{nq % 2}", lambda e, wb=wb, ei=ei, qt=qt: e.dma_start(
                            wb[:], w2[ei].rearrange("(c p) n -> p c n", p=128)
                            [:, qt * 8:(qt + 1) * 8, :]))
                        em.wait("tensor", f"d:wqs{nq % 2}")
                        em.wait("tensor", f"d:b2s{ei % 2}")
                        em.wait("tensor", "d:xn2")
                        em.wait("tensor", "scalar")
                        for tc in range(8):
                            em.wait_snap("tensor", "vector", "yps_free", ntc - 1)
                            if qt == 0:
                                for sub in range(2):
                                    for half in range(2):
                                        em.op("tensor", lambda e, sub=sub, half=half,
                                              bb=bb: e.matmul(
                                                  yps[sub][:, half * 512:(half + 1) * 512],
                                                  ones_r_b[:],
                                                  bb[:, half * 512:(half + 1) * 512],
                                                  start=True, stop=False))

                            for hc in range(8):
                                ap_ = aps[nhc % 2]
                                gp_ = gps[nhc % 2]
                                em.wait_snap("tensor", "vector", "ag_free", nhc - 2)
                                for c in range(8):
                                    em.op("tensor", lambda e, ap_=ap_, wa=wa, c=c,
                                          hc=hc, tc=tc: e.matmul(
                                              ap_[:], wa[:, c, hc * 128:(hc + 1) * 128],
                                              xnown[:, c, tc * 256:(tc + 1) * 256],
                                              start=(c == 0), stop=(c == 7)),
                                          tick=(c == 7))
                                for c in range(8):
                                    em.op("tensor", lambda e, gp_=gp_, wa=wa, c=c,
                                          hc=hc, tc=tc: e.matmul(
                                              gp_[:], wa[:, c, 1024 + hc * 128:
                                                          1024 + (hc + 1) * 128],
                                              xnown[:, c, tc * 256:(tc + 1) * 256],
                                              start=(c == 0), stop=(c == 7)),
                                          tick=(c == 7))
                                sg = sgb[nhc % 2]
                                ub = ubf[nhc % 2]
                                em.wait("scalar", "tensor")
                                em.wait_snap("scalar", "vector", "sg_used", nhc - 2)
                                em.op("scalar", lambda e, gp_=gp_, sg=sg, qt=qt,
                                      hc=hc: e.activation(
                                          sg[:], gp_[:], AF.Sigmoid,
                                          bias=b1tg[:, qt * 8 + hc:qt * 8 + hc + 1]),
                                      tick=True)
                                em.wait("vector", "scalar")
                                em.wait_snap("vector", "tensor", "ub_used", nhc - 2)
                                em.op("vector", lambda e, ap_=ap_, sg=sg, ub=ub,
                                      qt=qt, hc=hc: e.scalar_tensor_tensor(
                                          ub[:], ap_[:],
                                          b1ta[:, qt * 8 + hc:qt * 8 + hc + 1],
                                          sg[:], ALU.add, ALU.mult), tick=True)
                                em.snap("ag_free", "vector")
                                em.snap("sg_used", "vector")
                                em.wait("tensor", "vector")
                                for sub in range(2):
                                    for half in range(2):
                                        em.op("tensor", lambda e, sub=sub, half=half,
                                              ub=ub, wb=wb, hc=hc, qt=qt: e.matmul(
                                                  yps[sub][:, half * 512:(half + 1) * 512],
                                                  ub[:, sub * 128:(sub + 1) * 128],
                                                  wb[:, hc, half * 512:(half + 1) * 512],
                                                  start=(qt > 0 and hc == 0),
                                                  stop=(hc == 7)),
                                              tick=(hc == 7 and sub == 1 and half == 1))
                                em.snap("ub_used", "tensor")
                                nhc += 1
                            em.wait("vector", "tensor")
                            for sub in range(2):
                                slot = tc * 2 + sub
                                em.op("vector", lambda e, sub=sub, slot=slot,
                                      ei=ei: e.scalar_tensor_tensor(
                                          acctok[:, slot, :], yps[sub][:],
                                          maskS[:, slot * NE + ei:slot * NE + ei + 1],
                                          acctok[:, slot, :], ALU.mult, ALU.add),
                                      tick=(sub == 1))
                            em.snap("yps_free", "vector")
                            if qt == 0 and tc == 7:
                                em.snap("b2_used", "tensor")
                            ntc += 1
                        em.snap("wq_free", "tensor")
                        nq += 1

            em.run("ffn", ffn)

        with ExitStack() as es:
            lsb = lambda *a: es.enter_context(nc.sbuf_tensor(*a))
            xo = [lsb(f"xo{i}", [128, DM], F32) for i in range(2)]
            ob = [lsb(f"ob{i}", [128, DM], F32) for i in range(2)]

            def out_phase(em):
                for i in range(NSLOT):
                    xb = xo[i % 2]
                    em.wait_snap("sync", "vector", "xo_free", i - 2)
                    em.dma("sync", f"xo{i % 2}", lambda e, xb=xb, i=i: e.dma_start(
                        xb[:], xe[(4 * i) * 128:(4 * i + 1) * 128, :]))
                    o = ob[i % 2]
                    em.wait("vector", f"d:xo{i % 2}")
                    em.wait_snap("vector", f"d:yst{i % 2}", "ob_sent", i - 2)
                    em.op("vector", lambda e, xb=xb, o=o, i=i: e.tensor_tensor(
                        o[:], xb[:], acctok[:, i, :], ALU.add), tick=True)
                    em.snap("xo_free", "vector")
                    em.wait("sync", "vector")
                    em.dma("sync", f"yst{i % 2}", lambda e, o=o, i=i: e.dma_start(
                        y[i * 128:(i + 1) * 128, :], o[:]))
                    em.snap("ob_sent", f"d:yst{i % 2}")
                em.wait("sync", "d:yst0")
                em.wait("sync", "d:yst1")

            em.run("out", out_phase)
        stackAcc.close()

    return nc


def _perm_and_masks(j):
    sig = [j] + [m for m in range(4) if m != j]
    tile_order = []
    for u in range(16):
        for r in range(4):
            tile_order.append(4 * u + sig[r])
    m1 = np.zeros((128, 512), np.float32)
    qi = np.arange(128)[:, None]
    ki = np.arange(128)[None, :]
    for r in range(4):
        if r == 0:
            m1[:, 0:128] = np.where(ki > qi, NEG, 0.0)
        elif sig[r] > j:
            m1[:, r * 128:(r + 1) * 128] = NEG
    m2 = np.zeros((128, 512), np.float32)
    for t in range(4):
        m2[:, t * 128:(t + 1) * 128] = m1[:, t * 128:(t + 1) * 128].T
    return tile_order, m1, m2


def kernel(L0_attn_x, L0_attn_Wq, L0_attn_Wk, L0_attn_Wv, L0_attn_Wo,
           L0_attn_bo, L0_ffn_router_w, L0_ffn_router_b, L0_ffn_w1,
           L0_ffn_b1, L0_ffn_w2, L0_ffn_b2):
    x = np.ascontiguousarray(np.asarray(L0_attn_x, dtype=np.float32))
    args = dict(
        wq=np.ascontiguousarray(np.asarray(L0_attn_Wq, np.float32)),
        wk=np.ascontiguousarray(np.asarray(L0_attn_Wk, np.float32)),
        wv=np.ascontiguousarray(np.asarray(L0_attn_Wv, np.float32)),
        wo=np.ascontiguousarray(np.asarray(L0_attn_Wo, np.float32)),
        bo=np.ascontiguousarray(np.asarray(L0_attn_bo, np.float32)),
        rw=np.ascontiguousarray(np.asarray(L0_ffn_router_w, np.float32)),
        rb=np.ascontiguousarray(np.asarray(L0_ffn_router_b, np.float32)),
        w1=np.ascontiguousarray(np.asarray(L0_ffn_w1, np.float32)),
        b1=np.ascontiguousarray(np.asarray(L0_ffn_b1, np.float32)),
        w2=np.ascontiguousarray(np.asarray(L0_ffn_w2, np.float32)),
        b2=np.ascontiguousarray(np.asarray(L0_ffn_b2, np.float32)),
    )
    nc = build_nc()
    in_maps = []
    metas = []
    for c in range(8):
        b, j = c // 4, c % 4
        order, m1, m2 = _perm_and_masks(j)
        row_idx = np.concatenate(
            [np.arange(t * 128, (t + 1) * 128) for t in order])
        im = dict(args)
        im["xe"] = np.ascontiguousarray(x[b][row_idx])
        im["m1"] = m1
        im["m2"] = m2
        in_maps.append(im)
        metas.append((b, j))
    res = run_bass_kernel_spmd(nc, in_maps, core_ids=list(range(8)))
    out = np.empty((2, 8192, 1024), np.float32)
    for c in range(8):
        b, j = metas[c]
        yc = res.results[c]["y"]
        for i in range(NSLOT):
            t = 4 * i + j
            out[b, t * 128:(t + 1) * 128, :] = yc[i * 128:(i + 1) * 128, :]
    return out
